# version 45
# baseline (speedup 1.0000x reference)
import numpy as np
import concourse.bass as bass
import concourse.mybir as mybir

F32 = mybir.dt.float32
BF16 = mybir.dt.bfloat16
AF = mybir.ActivationFunctionType
ALU = mybir.AluOpType

D = 1024
KC = 8
H = 4096
HC = 32
EPS = 1e-5

SEM_NAMES = ("pe", "act", "dve", "pl",
             "dcst", "dsa", "dsb", "dsc", "dsd", "dx", "dxb", "dfa", "dfb", "dlna", "dlnb", "dx2a", "dx2b",
             "dv0", "dv1", "dv2", "dv3", "dv4", "dv5", "dv6", "dv7",
             "dvs0", "dvs1", "dxf0", "dxf1", "dln", "dod", "dwl", "dwb")


class Prog:
    ENGS = ("sync", "tensor", "vector", "scalar", "gpsimd")

    def __init__(self, sems, cnt, marks=None):
        self.sems = sems
        self.cnt = cnt
        self.marks = marks if marks is not None else {e: {} for e in self.ENGS}
        self.ops = {e: [] for e in self.ENGS}

    def wait(self, eng, evt):
        if evt is None:
            return
        sem_name, val = evt
        if val <= 0:
            return
        if self.marks[eng].get(sem_name, 0) >= val:
            return
        self.marks[eng][sem_name] = val
        s = self.sems[sem_name]
        self.ops[eng].append(lambda e, s=s, v=val: e.wait_ge(s, v))

    def op(self, eng, fn, inc=None, amt=1):
        if inc is None:
            self.ops[eng].append(fn)
            return None
        self.cnt[inc] += amt
        s = self.sems[inc]
        self.ops[eng].append(lambda e, fn=fn, s=s, a=amt: fn(e).then_inc(s, a))
        return (inc, self.cnt[inc])

    def dma(self, eng, fn, stream):
        return self.op(eng, fn, inc=stream, amt=16)

    def emit(self, nc):
        with nc.Block() as block:
            for eng in self.ENGS:
                lst = self.ops[eng]
                if not lst:
                    continue
                getattr(block, eng)(lambda e, lst=lst: [f(e) for f in lst])
        merged = {}
        for e in self.ENGS:
            for s, v in self.marks[e].items():
                if merged.get(s, 0) < v:
                    merged[s] = v
        for e in self.ENGS:
            self.marks[e] = dict(merged)


def build_nc(S=8192, dbg=False):
    NP1 = S // 256
    NQ = S // 256
    NP3 = S // 256

    nc = bass.Bass("TRN2", target_bir_lowering=False)

    x_in = nc.dram_tensor("x", [S, D], F32, kind="ExternalInput")
    Wq = nc.dram_tensor("Wq", [D, D], F32, kind="ExternalInput")
    Wk = nc.dram_tensor("Wk", [D, D], F32, kind="ExternalInput")
    Wv = nc.dram_tensor("Wv", [D, D], F32, kind="ExternalInput")
    bq_l = nc.dram_tensor("bq_l", [128, KC], F32, kind="ExternalInput")
    bk_l = nc.dram_tensor("bk_l", [128, KC], F32, kind="ExternalInput")
    bv_r = nc.dram_tensor("bv_r", [1, D], F32, kind="ExternalInput")
    w1 = nc.dram_tensor("w1", [D, H], F32, kind="ExternalInput")
    b1_l = nc.dram_tensor("b1_l", [128, HC], F32, kind="ExternalInput")
    w2 = nc.dram_tensor("w2", [H, D], F32, kind="ExternalInput")
    b2_r = nc.dram_tensor("b2_r", [1, D], F32, kind="ExternalInput")
    ident_in = nc.dram_tensor("ident", [128, 128], BF16, kind="ExternalInput")
    tri_in = nc.dram_tensor("tri", [128, 128], BF16, kind="ExternalInput")
    ones_c_in = nc.dram_tensor("ones_c", [128, 1], BF16, kind="ExternalInput")
    ones_r_in = nc.dram_tensor("ones_r", [1, 128], BF16, kind="ExternalInput")

    out = nc.dram_tensor("out", [S, D], F32, kind="ExternalOutput")
    sc_kind = "ExternalOutput" if dbg else "Internal"
    V_sc = nc.dram_tensor("V_sc", [S, D], BF16, kind=sc_kind)
    qfm_sc = nc.dram_tensor("qfm_sc", [128, NP1, KC, 256], BF16, kind=sc_kind)
    ln1_sc = nc.dram_tensor("ln1_sc", [S, D], F32, kind=sc_kind)
    w1bf_sc = nc.dram_tensor("w1bf_sc", [128, KC, H], BF16, kind="Internal")
    w2bf_sc = nc.dram_tensor("w2bf_sc", [128, HC, D], BF16, kind="Internal")
    z_dbg = nc.dram_tensor("z_dbg", [S, D], F32, kind=sc_kind) if dbg else None
    rl_dbg = nc.dram_tensor("rl_dbg", [S // 256, 128, 4], F32, kind=sc_kind) if dbg else None

    x_t = x_in.rearrange("(n s p) d -> n p s d", p=128, s=2)
    v_t = V_sc.rearrange("(n p) d -> n p d", p=128)
    ln_t = ln1_sc.rearrange("(n s p) d -> n p s d", p=128, s=2)
    z_t = z_dbg.rearrange("(n s p) d -> n p s d", p=128, s=2) if dbg else None
    out_t = out.rearrange("(n s p) d -> n p s d", p=128, s=2)

    cnt = {s: 0 for s in SEM_NAMES}
    marks = {e: {} for e in Prog.ENGS}

    from contextlib import ExitStack
    es = ExitStack()
    with es:
        sems = {s: es.enter_context(nc.semaphore(f"s_{s}")) for s in SEM_NAMES}
        ident = es.enter_context(nc.sbuf_tensor("ident_sb", [128, 128], BF16))
        tri = es.enter_context(nc.sbuf_tensor("tri_sb", [128, 128], BF16))
        ones_c = es.enter_context(nc.sbuf_tensor("ones_c_sb", [128, 1], BF16))
        ones_r33 = es.enter_context(nc.sbuf_tensor("ones_r_sb", [33, 128], BF16))
        ones_r = ones_r33[0:1, :]
        ones_r32 = ones_r33[32:33, :]
        bq_sb = es.enter_context(nc.sbuf_tensor("bq_sb", [128, KC], F32))
        bk_sb = es.enter_context(nc.sbuf_tensor("bk_sb", [128, KC], F32))
        b1_sb = es.enter_context(nc.sbuf_tensor("b1_sb", [128, HC], F32))
        bvb2_bf = es.enter_context(nc.sbuf_tensor("bvb2_bf", [33, D], BF16))
        bv_bf = bvb2_bf[0:1, :]
        b2_bf = bvb2_bf[32:33, :]
        stats = es.enter_context(nc.sbuf_tensor("stats", [128, 8], F32))
        stats2 = es.enter_context(nc.sbuf_tensor("stats2", [128, 8], F32))
        nv_sb = es.enter_context(nc.sbuf_tensor("nv_sb", [128, 4], F32))
        sd_sb = es.enter_context(nc.sbuf_tensor("sd_sb", [128, 4], F32))
        rs_sb = es.enter_context(nc.sbuf_tensor("rs_sb", [128, 4], F32))
        rl_sb = es.enter_context(nc.sbuf_tensor("rl_sb", [128, 4], F32))
        eps_sb = es.enter_context(nc.sbuf_tensor("eps_sb", [128, 1], F32))

        ev = {}

        def prev2(name, i, c, per):
            if c - 2 >= 0:
                return ev.get((name, i, c - 2))
            if i < 0 or i == 0:
                return None
            return ev.get((name, i - 1, per + (c - 2)))

        with ExitStack() as ab:
            K_fm = ab.enter_context(nc.sbuf_tensor("K_fm", [128, KC, S], BF16))
            WqT = ab.enter_context(nc.sbuf_tensor("WqT", [128, KC, D], BF16))

            with ExitStack() as a:
                WkT = a.enter_context(nc.sbuf_tensor("WkT", [128, KC, D], BF16))
                WvT = a.enter_context(nc.sbuf_tensor("WvT", [128, KC, D], BF16))
                bv_bc = a.enter_context(nc.sbuf_tensor("bv_bc", [128, D], BF16))
                slab_scope = ExitStack()
                slab = [slab_scope.enter_context(nc.sbuf_tensor(f"slab{j}", [128, KC, 128], F32)) for j in range(3)]
                slab_bf = [slab_scope.enter_context(nc.sbuf_tensor(f"slabbf{j}", [128, KC, 128], BF16)) for j in range(3)]
                bvb2_f32 = slab_scope.enter_context(nc.sbuf_tensor("bvb2_f32", [33, D], F32))
                bv_f32 = bvb2_f32[0:1, :]
                b2_f32 = bvb2_f32[32:33, :]
                ps_t = [a.enter_context(nc.psum_tensor(f"ps_t{j}", [128, 1024], BF16)) for j in range(2)]
                ps_k = [a.enter_context(nc.psum_tensor(f"ps_k{j}", [128, 512], F32)) for j in range(2)]
                ps_v = [a.enter_context(nc.psum_tensor(f"ps_v{j}", [128, 1024], F32)) for j in range(2)]

                p0 = Prog(sems, cnt, marks)
                for dst, src in ((ident, ident_in), (tri, tri_in), (ones_c, ones_c_in),
                                 (ones_r, ones_r_in), (ones_r32, ones_r_in),
                                 (bq_sb, bq_l), (bk_sb, bk_l),
                                 (b1_sb, b1_l), (bv_f32, bv_r), (b2_f32, b2_r)):
                    ev["consts"] = p0.dma("sync", lambda e, d=dst, s=src: e.dma_start(d[:, :], s[:, :]), "dcst")
                p0.wait("vector", ev["consts"])
                p0.op("vector", lambda e: e.memset(eps_sb[:, :], EPS))
                p0.op("vector", lambda e: e.tensor_copy(bv_bf[:, :], bv_f32[:, :]))
                ev["bcast"] = p0.op("vector", lambda e: e.tensor_copy(b2_bf[:, :], b2_f32[:, :]), inc="dve")
                p0.wait("tensor", ev["bcast"])
                for oh in range(2):
                    fn = lambda e, oh=oh: e.matmul(
                        ps_v[0][:, oh * 512:(oh + 1) * 512], ones_r[:, :],
                        bv_bf[:, oh * 512:(oh + 1) * 512], start=True, stop=True)
                    ev["bvmm"] = p0.op("tensor", fn, inc="pe" if oh == 1 else None)
                p0.wait("scalar", ev["bvmm"])
                ev["bvbc"] = p0.op(
                    "scalar", lambda e: e.activation(bv_bc[:, :], ps_v[0][:, :], AF.Identity),
                    inc="act")

                def w_prep(p, Wsrc, Wdst, rounds):
                    for r in range(KC):
                        rg = len(rounds)
                        par = rg % 3
                        prev = rounds[rg - 3] if rg >= 3 else {}
                        ldeng = "sync"
                        p.wait(ldeng, prev.get("cast"))
                        e_ld = p.dma(
                            ldeng,
                            lambda e, r=r, par=par, W=Wsrc: e.dma_start(
                                slab[par][:, :, :].rearrange("p a b -> p (a b)"),
                                W[r * 128:(r + 1) * 128, :]),
                            ("dsa", "dsb", "dsc")[par])
                        p.wait("vector", e_ld)
                        p.wait("vector", prev.get("tr"))
                        e_cast = p.op(
                            "vector", lambda e, par=par: e.tensor_copy(slab_bf[par][:, :, :], slab[par][:, :, :]),
                            inc="dve")
                        p.wait("tensor", e_cast)
                        p.wait("tensor", (rounds[rg - 2] if rg >= 2 else {}).get("evac"))
                        e_tr = None
                        for k in range(KC):
                            fn = lambda e, k=k, par=par: e.transpose(
                                ps_t[par % 2][:, k * 128:(k + 1) * 128],
                                slab_bf[par][:, :, :].rearrange("p a b -> p (a b)")[:, k * 128:(k + 1) * 128],
                                ident[:, :])
                            if k == KC - 1:
                                e_tr = p.op("tensor", fn, inc="pe")
                            else:
                                p.op("tensor", fn)
                        p.wait("vector", e_tr)
                        e_evac = p.op(
                            "vector", lambda e, r=r, par=par, Wd=Wdst: e.tensor_copy(
                                Wd[:, :, r * 128:(r + 1) * 128],
                                ps_t[par % 2][:, :].rearrange("p (k t) -> p k t", k=KC)),
                            inc="dve")
                        rounds.append({"cast": e_cast, "tr": e_tr, "evac": e_evac})
                w_rounds = []
                w_prep(p0, Wk, WkT, w_rounds)
                w_prep(p0, Wv, WvT, w_rounds)
                w_prep(p0, Wq, WqT, w_rounds)
                p0.emit(nc)
                slab_scope.close()
                x_sb = a.enter_context(nc.sbuf_tensor("x_sb", [128, 2, D], F32))
                xbf = a.enter_context(nc.sbuf_tensor("xbf", [128, 2, D], BF16))
                x_fm = [a.enter_context(nc.sbuf_tensor(f"x_fm{j}", [128, KC, 256], BF16)) for j in range(2)]
                v_stage = [a.enter_context(nc.sbuf_tensor(f"v_st{j}", [128, D], BF16)) for j in range(2)]

                p1 = Prog(sems, cnt, marks)

                def p1_xld_cast(i):
                    if i >= NP1:
                        return
                    p1.wait("sync", ev.get(("cast1", i - 1)))
                    ev[("xld", i)] = p1.dma(
                        "sync", lambda e, i=i: e.dma_start(x_sb[:, :, :], x_t[i]), "dx")
                    p1.wait("vector", ev[("xld", i)])
                    p1.wait("vector", ev.get(("transp", i - 1, 1)))
                    ev[("cast1", i)] = p1.op(
                        "vector", lambda e: e.tensor_copy(
                            xbf[:, :, :].rearrange("p s d -> p (s d)"),
                            x_sb[:, :, :].rearrange("p s d -> p (s d)")), inc="dve")

                def p1_prep(j):
                    if j >= NP1:
                        return
                    parj = j % 2
                    for s2 in range(2):
                        p1.wait("tensor", ev[("cast1", j)])
                        p1.wait("tensor", ev.get(("tevac", j - 1, s2)))
                        for k in range(KC):
                            fn = lambda e, s2=s2, k=k: e.transpose(
                                ps_t[s2][:, k * 128:(k + 1) * 128], xbf[:, s2, k * 128:(k + 1) * 128], ident[:, :])
                            if k == KC - 1:
                                ev[("transp", j, s2)] = p1.op("tensor", fn, inc="pe")
                            else:
                                p1.op("tensor", fn)
                        p1.wait("vector", ev[("transp", j, s2)])
                        p1.wait("vector", ev.get(("vproj", j - 2, 1)))
                        ev[("tevac", j, s2)] = p1.op(
                            "vector", lambda e, s2=s2, parj=parj: e.tensor_copy(
                                x_fm[parj][:, :, s2 * 128:(s2 + 1) * 128],
                                ps_t[s2][:, :].rearrange("p (k t) -> p k t", k=KC)), inc="dve")

                p1_xld_cast(0)
                p1_prep(0)
                for i in range(NP1):
                    par = i % 2
                    p1_xld_cast(i + 1)
                    for c in range(2 * KC):
                        if c == 8:
                            p1_prep(i + 1)
                        if c == 0:
                            p1.wait("tensor", ev[("tevac", i, 1)])
                        p1.wait("tensor", prev2("kevac", i, c, 2 * KC))
                        WT = WkT if c < KC else WqT
                        cc = c % KC
                        for k in range(KC):
                            fn = lambda e, c=c, cc=cc, k=k, par=par, WT=WT: e.matmul(
                                ps_k[c % 2][:, 0:256], WT[:, k, cc * 128:(cc + 1) * 128],
                                x_fm[par][:, k, :], start=(k == 0), stop=(k == KC - 1))
                            if k == KC - 1:
                                ev[("kproj", i, c)] = p1.op("tensor", fn, inc="pe")
                            else:
                                p1.op("tensor", fn)
                        p1.wait("scalar", ev[("kproj", i, c)])
                        if c < KC:
                            ev[("kevac", i, c)] = p1.op(
                                "scalar", lambda e, i=i, c=c: e.activation(
                                    K_fm[:, c, i * 256:(i + 1) * 256], ps_k[c % 2][:, 0:256],
                                    AF.Identity, bias=bk_sb[:, c:c + 1], scale=1.0), inc="act")
                        else:
                            hh = cc // 4
                            if cc % 4 == 0:
                                p1.wait("scalar", ev.get(("vdma", i - 1, hh)))
                            ev[("kevac", i, c)] = p1.op(
                                "scalar", lambda e, c=c, cc=cc, hh=hh: e.activation(
                                    v_stage[hh][:, (cc % 4) * 256:(cc % 4 + 1) * 256],
                                    ps_k[c % 2][:, 0:256],
                                    AF.Identity, bias=bq_sb[:, cc:cc + 1], scale=1.0), inc="act")
                            if cc % 4 == 3:
                                p1.wait("scalar", ev[("kevac", i, c)])
                                ev[("qdma", i, hh)] = p1.dma(
                                    "scalar", lambda e, i=i, hh=hh: e.dma_start(
                                        qfm_sc[:, i, 4 * hh:4 * hh + 4, :],
                                        v_stage[hh][:, :].rearrange("p (a b) -> p a b", a=4)),
                                    "dxf0" if hh == 0 else "dxf1")
                    for s2 in range(2):
                        p1.wait("tensor", ev.get(("vevac", i - 1, s2)))
                        for oh in range(2):
                            for k in range(KC):
                                fn = lambda e, s2=s2, oh=oh, k=k, par=par: e.matmul(
                                    ps_v[s2][:, oh * 512:(oh + 1) * 512],
                                    x_fm[par][:, k, s2 * 128:(s2 + 1) * 128],
                                    WvT[:, k, oh * 512:(oh + 1) * 512],
                                    start=(k == 0), stop=(k == KC - 1))
                                if oh == 1 and k == KC - 1:
                                    ev[("vproj", i, s2)] = p1.op("tensor", fn, inc="pe")
                                else:
                                    p1.op("tensor", fn)
                        p1.wait("scalar", ev[("vproj", i, s2)])
                        p1.wait("scalar", ev.get(("vdma", i - 1, s2)))
                        p1.wait("scalar", ev[("qdma", i, s2)])
                        ev[("vevac", i, s2)] = p1.op(
                            "scalar", lambda e, s2=s2: e.activation(
                                v_stage[s2][:, :], ps_v[s2][:, :], AF.Identity), inc="act")
                        p1.wait("gpsimd", ev[("vevac", i, s2)])
                        ev[("bvadd", i, s2)] = p1.op(
                            "gpsimd", lambda e, s2=s2: e.tensor_tensor(
                                v_stage[s2][:, :], v_stage[s2][:, :], bv_bc[:, :], ALU.add),
                            inc="pl")
                        p1.wait("gpsimd", ev[("bvadd", i, s2)])
                        ev[("vdma", i, s2)] = p1.dma(
                            "gpsimd", lambda e, i=i, s2=s2: e.dma_start(
                                v_t[2 * i + s2], v_stage[s2][:, :]), "dvs0" if s2 == 0 else "dvs1")
                for s in ("dvs0", "dvs1", "dxf0", "dxf1"):
                    p1.wait("scalar", (s, cnt[s]))
                p1.emit(nc)

            with ExitStack() as b:
                x2_sb = [b.enter_context(nc.sbuf_tensor(f"x2_sb{j}", [128, 2, D], F32)) for j in range(2)]
                Q_sb = [b.enter_context(nc.sbuf_tensor(f"Q_sb{j}", [128, KC, 256], BF16)) for j in range(2)]
                P_sb = [b.enter_context(nc.sbuf_tensor(f"P_sb{j}", [128, 256], BF16)) for j in range(4)]
                V_ring = [b.enter_context(nc.sbuf_tensor(f"V_ring{j}", [128, D], BF16)) for j in range(8)]
                ln1_sb = b.enter_context(nc.sbuf_tensor("ln1_sb", [128, 2, D], F32))
                wsl = b.enter_context(nc.sbuf_tensor("wsl", [128, 1024], F32))
                wslb = b.enter_context(nc.sbuf_tensor("wslb", [128, 1024], BF16))
                ps_s = [b.enter_context(nc.psum_tensor(f"ps_s{j}", [128, 512], F32)) for j in range(2)]
                ps_z = [b.enter_context(nc.psum_tensor(f"ps_z{j}", [128, 1024], F32)) for j in range(2)]
                ps_l = b.enter_context(nc.psum_tensor("ps_l", [128, 512], F32))

                p2 = Prog(sems, cnt, marks)
                g = 0
                gev = {}

                def p2_loads(i):
                    if i >= NQ:
                        return
                    par = i % 2
                    p2.wait("gpsimd", ev.get(("tdone", i - 2)))
                    ev[("qfm", i)] = p2.dma(
                        "gpsimd", lambda e, i=i, par=par: e.dma_start(
                            Q_sb[par][:, :, :], qfm_sc[:, i, :, :]), "dfa" if par == 0 else "dfb")
                    p2.wait("gpsimd", ev.get(("apply", i - 2)))
                    ev[("x2", i)] = p2.dma(
                        "gpsimd", lambda e, i=i, par=par: e.dma_start(x2_sb[par][:, :, :], x_t[i]),
                        "dx2a" if par == 0 else "dx2b")

                pending2 = []

                def p2_tail(i):
                    par_t = i % 2
                    p2.wait("scalar", ev[("nv", i)])
                    ev[("sd", i)] = p2.op(
                        "scalar", lambda e: e.activation(
                            sd_sb[:, 0:2], nv_sb[:, 0:2], AF.Ln, bias=eps_sb[:, 0:1], scale=-1.0), inc="act")
                    p2.wait("scalar", ev[("sd", i)])
                    ev[("rs", i)] = p2.op(
                        "scalar", lambda e: e.activation(
                            rs_sb[:, 0:2], sd_sb[:, 0:2], AF.Exp, scale=-0.5), inc="act")
                    p2.wait("vector", ev[("rs", i)])
                    for j in range(2):
                        fn = lambda e, j=j, par_t=par_t: e.tensor_scalar(
                            ln1_sb[:, j, :], x2_sb[par_t][:, j, :], stats2[:, j:j + 1], rs_sb[:, j:j + 1],
                            ALU.subtract, ALU.mult)
                        if j == 1:
                            ev[("apply", i)] = p2.op("vector", fn, inc="dve")
                        else:
                            p2.op("vector", fn)
                    p2.wait("gpsimd", ev[("apply", i)])
                    ev[("lndma", i)] = p2.dma(
                        "gpsimd", lambda e, i=i: e.dma_start(ln_t[i], ln1_sb[:, :, :]), "dln")

                p2_loads(0)

                for i in range(NQ):
                    par = i % 2
                    d0 = 2 * i
                    nch = 2 * i + 2

                    def v_load(c):
                        gg = g + c
                        p2.wait("sync", gev.get(("pv", gg - 8)))
                        gev[("vld", gg)] = p2.dma(
                            "sync", lambda e, c=c, gg=gg: e.dma_start(
                                V_ring[gg % 8][:, :], v_t[c]), f"dv{gg % 8}")
                    for c in range(min(8, nch)):
                        v_load(c)

                    def scores(c):
                        gg = g + c
                        qlo = 0 if c <= d0 else 128
                        if c == 0:
                            p2.wait("tensor", ev[("qfm", i)])
                        p2.wait("tensor", gev.get(("exp", gg - 2)))
                        for k in range(KC):
                            fn = lambda e, c=c, k=k, qlo=qlo, par=par, gg=gg: e.matmul(
                                ps_s[gg % 2][:, qlo:256], K_fm[:, k, c * 128:(c + 1) * 128],
                                Q_sb[par][:, k, qlo:256], start=(k == 0), stop=(k == KC - 1))
                            if k == KC - 1:
                                gev[("sc", gg)] = p2.op("tensor", fn, inc="pe")
                            else:
                                p2.op("tensor", fn)
                        p2.wait("scalar", gev[("sc", gg)])
                        p2.wait("scalar", gev.get(("pv", gg - 4)))
                        gev[("exp", gg)] = p2.op(
                            "scalar", lambda e, gg=gg, qlo=qlo: e.activation(
                                P_sb[gg % 4][:, qlo:256], ps_s[gg % 2][:, qlo:256],
                                AF.Exp, scale=1.0 / 32.0), inc="act")
                        if c >= d0:
                            j = c - d0
                            p2.wait("vector", gev[("exp", gg)])
                            gev[("mask", gg)] = p2.op(
                                "vector", lambda e, gg=gg, j=j: e.tensor_mul(
                                    P_sb[gg % 4][:, j * 128:(j + 1) * 128],
                                    P_sb[gg % 4][:, j * 128:(j + 1) * 128], tri[:, :]), inc="dve")

                    def pv(c):
                        gg = g + c
                        for j in range(2):
                            if c > d0 + j:
                                continue
                            if c - d0 == j:
                                p2.wait("tensor", gev[("mask", gg)])
                            else:
                                p2.wait("tensor", gev[("exp", gg)])
                            p2.wait("tensor", gev[("vld", gg)])
                            if c == 0:
                                p2.wait("tensor", ev.get((f"z{j}", i - 1)))
                            lhs = lambda gg=gg, j=j: P_sb[gg % 4][:, j * 128:(j + 1) * 128]
                            stz = (c == d0 + j)
                            stl = (c == d0 + 1 and j == 1)
                            for oh in range(2):
                                p2.op("tensor", lambda e, gg=gg, j=j, oh=oh, c=c, lhs=lhs, stz=stz: e.matmul(
                                    ps_z[j][:, oh * 512:(oh + 1) * 512], lhs(),
                                    V_ring[gg % 8][:, oh * 512:(oh + 1) * 512],
                                    start=(c == 0), stop=stz))
                            fn = lambda e, gg=gg, j=j, c=c, lhs=lhs, stl=stl: e.matmul(
                                ps_l[:, j:j + 1], lhs(), ones_c[:, :],
                                start=(c == 0 and j == 0), stop=stl)
                            if j == 1:
                                gev[("pv", gg)] = p2.op("tensor", fn, inc="pe")
                            else:
                                p2.op("tensor", fn)

                    for c in range(nch):
                        if c == 3 and pending2:
                            p2_tail(pending2.pop(0))
                        if c == 4:
                            p2_loads(i + 1)
                        scores(c)
                        if c > 0:
                            pv(c - 1)
                            if c + 7 < nch:
                                v_load(c + 7)
                    if pending2:
                        p2_tail(pending2.pop(0))
                    if nch <= 4:
                        p2_loads(i + 1)
                    pv(nch - 1)
                    g += nch
                    ev[("tdone", i)] = gev[("pv", g - 1)]

                    p2.wait("vector", gev[("pv", g - 1)])
                    ev[("rl", i)] = p2.op(
                        "vector", lambda e: e.reciprocal(rl_sb[:, 0:2], ps_l[:, 0:2]), inc="dve")
                    p2.wait("vector", ev[("rl", i)])
                    p2.wait("vector", ev[("x2", i)])
                    for j in range(2):
                        fn = lambda e, j=j, par=par: e.scalar_tensor_tensor(
                            x2_sb[par][:, j, :], ps_z[j][:, :], rl_sb[:, j:j + 1], x2_sb[par][:, j, :],
                            ALU.mult, ALU.add, accum_out=stats[:, j:j + 1])
                        ev[(f"z{j}", i)] = p2.op("vector", fn, inc="dve")
                    if dbg:
                        p2.wait("scalar", ev[("z1", i)])
                        p2.dma("scalar", lambda e, i=i, par=par: e.dma_start(z_t[i], x2_sb[par][:, :, :]), "dod")
                        p2.dma("scalar", lambda e, i=i: e.dma_start(rl_dbg[i], rl_sb[:, :]), "dod")
                    p2.wait("vector", ev[("z1", i)])
                    p2.wait("vector", ev.get(("lndma", i - 1)))
                    for j in range(2):
                        fn = lambda e, j=j, par=par: e.scalar_tensor_tensor(
                            ln1_sb[:, j, :], x2_sb[par][:, j, :], 1.0, x2_sb[par][:, j, :],
                            ALU.mult, ALU.mult, accum_out=stats[:, 2 + j:3 + j])
                        if j == 1:
                            ev[("sq", i)] = p2.op("vector", fn, inc="dve")
                        else:
                            p2.op("vector", fn)
                    p2.wait("vector", ev[("sq", i)])
                    ev[("mt", i)] = p2.op(
                        "vector", lambda e: e.tensor_scalar(
                            stats2[:, 0:4], stats[:, 0:4], 1.0 / D, None, ALU.mult), inc="dve")
                    p2.wait("vector", ev[("mt", i)])
                    for j in range(2):
                        fn = lambda e, j=j: e.scalar_tensor_tensor(
                            nv_sb[:, j:j + 1], stats2[:, j:j + 1], stats2[:, j:j + 1],
                            stats2[:, 2 + j:3 + j], ALU.mult, ALU.subtract)
                        if j == 1:
                            ev[("nv", i)] = p2.op("vector", fn, inc="dve")
                        else:
                            p2.op("vector", fn)
                    pending2.append(i)
                    rpt = -(-64 // NQ)
                    for r in range(rpt * i, rpt * (i + 1)):
                        if r < 32:
                            k_, q_ = r // 4, r % 4
                            wsrc = w1[k_ * 128:(k_ + 1) * 128, q_ * 1024:(q_ + 1) * 1024]
                            wdst = w1bf_sc[:, k_, q_ * 1024:(q_ + 1) * 1024]
                        elif r < 64:
                            hc_ = r - 32
                            wsrc = w2[hc_ * 128:(hc_ + 1) * 128, :]
                            wdst = w2bf_sc[:, hc_, :]
                        else:
                            continue
                        p2.wait("sync", ev.get(("wcst1", r - 1)))
                        ev[("wld1", r)] = p2.dma(
                            "sync", lambda e, wsrc=wsrc: e.dma_start(wsl[:, :], wsrc), "dwl")
                        p2.wait("vector", ev[("wld1", r)])
                        p2.wait("vector", ev.get(("wst", r - 1)))
                        ev[("wcst1", r)] = p2.op(
                            "vector", lambda e: e.tensor_copy(wslb[:, :], wsl[:, :]), inc="dve")
                        p2.wait("gpsimd", ev[("wcst1", r)])
                        ev[("wst", r)] = p2.dma(
                            "gpsimd", lambda e, wdst=wdst: e.dma_start(wdst, wslb[:, :]), "dwb")
                while pending2:
                    p2_tail(pending2.pop(0))
                p2.wait("gpsimd", ("dln", cnt["dln"]))
                p2.wait("gpsimd", ("dwb", cnt["dwb"]))
                p2.emit(nc)

        with ExitStack() as cs:
            w1_sb = cs.enter_context(nc.sbuf_tensor("w1_sb", [128, KC, H], BF16))
            w2_sb = cs.enter_context(nc.sbuf_tensor("w2_sb", [128, HC, D], BF16))
            b2_bc = cs.enter_context(nc.sbuf_tensor("b2_bc", [128, D], F32))
            ln_sb = [cs.enter_context(nc.sbuf_tensor(f"ln_sb{j}", [128, 2, D], F32)) for j in range(2)]
            lnbf = cs.enter_context(nc.sbuf_tensor("lnbf", [128, 2, D], BF16))
            lnfm = [cs.enter_context(nc.sbuf_tensor(f"lnfm{j}", [128, KC, 256], BF16)) for j in range(2)]
            H_sb = cs.enter_context(nc.sbuf_tensor("H_sb", [128, HC, 256], BF16))
            out_sb = cs.enter_context(nc.sbuf_tensor("out_sb", [128, 2, D], F32))
            ps_t3 = [cs.enter_context(nc.psum_tensor(f"ps_t3{j}", [128, 1024], BF16)) for j in range(2)]
            ps_h = [cs.enter_context(nc.psum_tensor(f"ps_h{j}", [128, 512], F32)) for j in range(2)]
            ps_y = [cs.enter_context(nc.psum_tensor(f"ps_y{j}", [128, 1024], F32)) for j in range(2)]

            p3 = Prog(sems, cnt, marks)
            ev[("lnld", 0)] = p3.dma(
                "sync", lambda e: e.dma_start(ln_sb[0][:, :, :], ln_t[0]), "dlna")
            for oh in range(2):
                fn = lambda e, oh=oh: e.matmul(
                    ps_y[0][:, oh * 512:(oh + 1) * 512], ones_r32[:, :],
                    b2_bf[:, oh * 512:(oh + 1) * 512], start=True, stop=True)
                ev["b2mm"] = p3.op("tensor", fn, inc="pe" if oh == 1 else None)
            p3.wait("vector", ev["b2mm"])
            ev["b2bc"] = p3.op(
                "vector", lambda e: e.tensor_copy(b2_bc[:, :], ps_y[0][:, :]), inc="dve")
            p3.wait("sync", ("dwb", cnt["dwb"]))
            for r in range(8):
                dstream = ("dsa", "dsb", "dsc", "dsd", "dwl", "dwb", "dcst", "dx")[r]
                if r >= 2:
                    p3.wait("sync", ev[("wld3", r - 2)])
                if r < 4:
                    fn = lambda e, r=r: e.dma_start(
                        w1_sb[:, :, r * 1024:(r + 1) * 1024],
                        w1bf_sc[:, :, r * 1024:(r + 1) * 1024])
                else:
                    g_ = r - 4
                    fn = lambda e, g_=g_: e.dma_start(
                        w2_sb[:, 8 * g_:8 * g_ + 8, :], w2bf_sc[:, 8 * g_:8 * g_ + 8, :])
                ev[("wld3", r)] = p3.dma("sync", fn, dstream)

            def p3_lnld(i):
                if i >= NP3:
                    return
                par = i % 2
                dstream = "dlna" if par == 0 else "dlnb"
                p3.wait("sync", ev.get(("z3", i - 2)))
                ev[("lnld", i)] = p3.dma(
                    "sync", lambda e, i=i, par=par: e.dma_start(ln_sb[par][:, :, :], ln_t[i]),
                    dstream)

            def p3_prep(i):
                if i >= NP3:
                    return
                par = i % 2
                p3.wait("vector", ev[("lnld", i)])
                p3.wait("vector", ev.get(("t3", i - 1, 1)))
                ev[("cast3", i)] = p3.op(
                    "vector", lambda e, par=par: e.tensor_copy(
                        lnbf[:, :, :].rearrange("p s d -> p (s d)"),
                        ln_sb[par][:, :, :].rearrange("p s d -> p (s d)")), inc="dve")
                p3.wait("vector", ev[("cast3", i)])
                if i == 0:
                    p3.wait("vector", ev["b2bc"])
                for j in range(2):
                    fn = lambda e, j=j, par=par: e.tensor_tensor(
                        ln_sb[par][:, j, :], ln_sb[par][:, j, :], b2_bc[:, :], ALU.add)
                    if j == 1:
                        ev[("b2add", i)] = p3.op("vector", fn, inc="dve")
                    else:
                        p3.op("vector", fn)
                for s2 in range(2):
                    p3.wait("tensor", ev[("cast3", i)])
                    p3.wait("tensor", ev.get(("t3evac", i - 1, s2)))
                    for k in range(KC):
                        fn = lambda e, s2=s2, k=k: e.transpose(
                            ps_t3[s2][:, k * 128:(k + 1) * 128], lnbf[:, s2, k * 128:(k + 1) * 128], ident[:, :])
                        if k == KC - 1:
                            ev[("t3", i, s2)] = p3.op("tensor", fn, inc="pe")
                        else:
                            p3.op("tensor", fn)
                    p3.wait("vector", ev[("t3", i, s2)])
                    ev[("t3evac", i, s2)] = p3.op(
                        "vector", lambda e, s2=s2, par=par: e.tensor_copy(
                            lnfm[par][:, :, s2 * 128:(s2 + 1) * 128],
                            ps_t3[s2][:, :].rearrange("p (k t) -> p k t", k=KC)), inc="dve")

            pending_tail = []

            def p3_tail(i):
                p3.wait("scalar", ev[("nv3", i)])
                ev[("sd3", i)] = p3.op(
                    "scalar", lambda e: e.activation(
                        sd_sb[:, 2:4], nv_sb[:, 2:4], AF.Sqrt, bias=eps_sb[:, 0:1], scale=-1.0), inc="act")
                p3.op("scalar", lambda e: e.activation(
                    sd_sb[:, 0:1], eps_sb[:, 0:1], AF.Gelu))
                p3.wait("vector", ev[("sd3", i)])
                ev[("rs3", i)] = p3.op(
                    "vector", lambda e: e.reciprocal(rs_sb[:, 2:4], sd_sb[:, 2:4]), inc="dve")
                p3.wait("vector", ev[("rs3", i)])
                for j in range(2):
                    fn = lambda e, j=j: e.tensor_scalar(
                        out_sb[:, j, :], out_sb[:, j, :], stats2[:, 4 + j:5 + j], rs_sb[:, 2 + j:3 + j],
                        ALU.subtract, ALU.mult)
                    if j == 1:
                        ev[("out3", i)] = p3.op("vector", fn, inc="dve")
                    else:
                        p3.op("vector", fn)
                p3.wait("scalar", ev[("out3", i)])
                ev[("odma", i)] = p3.dma(
                    "scalar", lambda e, i=i: e.dma_start(out_t[i], out_sb[:, :, :]), "dod")

            p3_prep(0)

            for i in range(NP3):
                par = i % 2
                p3_lnld(i + 1)
                for hc in range(HC):
                    if hc == 0:
                        p3.wait("tensor", ev[("t3evac", i, 1)])
                    p3.wait("tensor", prev2("gelu", i, hc, HC))
                    if i == 0:
                        p3.wait("tensor", ev[("wld3", hc // 8)])
                    for k in range(KC):
                        fn = lambda e, hc=hc, k=k, par=par: e.matmul(
                            ps_h[hc % 2][:, 0:256], w1_sb[:, k, hc * 128:(hc + 1) * 128],
                            lnfm[par][:, k, :], start=(k == 0), stop=(k == KC - 1))
                        if k == KC - 1:
                            ev[("f1", i, hc)] = p3.op("tensor", fn, inc="pe")
                        else:
                            p3.op("tensor", fn)
                    p3.wait("scalar", ev[("f1", i, hc)])
                    if hc == 0:
                        p3.wait("scalar", ev.get(("f2", i - 1, 1)))
                    ev[("gelu", i, hc)] = p3.op(
                        "scalar", lambda e, hc=hc: e.activation(
                            H_sb[:, hc, :], ps_h[hc % 2][:, 0:256], AF.Gelu,
                            bias=b1_sb[:, hc:hc + 1], scale=1.0), inc="act")
                p3_prep(i + 1)
                if pending_tail:
                    p3_tail(pending_tail.pop(0))
                for j in range(2):
                    p3.wait("tensor", ev[("gelu", i, HC - 1)])
                    p3.wait("tensor", ev.get(("z3", i - 1)))
                    if i == 0 and j == 0:
                        p3.wait("tensor", ev["b2bc"])
                    for hc in range(HC):
                        if i == 0 and j == 0:
                            p3.wait("tensor", ev[("wld3", 4 + hc // 8)])
                        for oh in range(2):
                            fn = lambda e, j=j, hc=hc, oh=oh: e.matmul(
                                ps_y[j][:, oh * 512:(oh + 1) * 512], H_sb[:, hc, j * 128:(j + 1) * 128],
                                w2_sb[:, hc, oh * 512:(oh + 1) * 512],
                                start=(hc == 0), stop=(hc == HC - 1))
                            if hc == HC - 1 and oh == 1:
                                ev[("f2", i, j)] = p3.op("tensor", fn, inc="pe")
                            else:
                                p3.op("tensor", fn)
                p3.wait("vector", ev[("f2", i, 1)])
                p3.wait("vector", ev.get(("odma", i - 1)))
                p3.wait("vector", ev[("b2add", i)])
                for j in range(2):
                    fn = lambda e, j=j, par=par: e.scalar_tensor_tensor(
                        out_sb[:, j, :], ps_y[j][:, :], 1.0, ln_sb[par][:, j, :],
                        ALU.mult, ALU.add, accum_out=stats[:, 4 + j:5 + j])
                    if j == 1:
                        ev[("z3", i)] = p3.op("vector", fn, inc="dve")
                    else:
                        p3.op("vector", fn)
                p3.wait("vector", ev[("z3", i)])
                for j in range(2):
                    fn = lambda e, j=j: e.scalar_tensor_tensor(
                        lnbf[:, j, :], out_sb[:, j, :], 1.0, out_sb[:, j, :],
                        ALU.mult, ALU.mult, accum_out=stats[:, 6 + j:7 + j])
                    if j == 1:
                        ev[("sq3", i)] = p3.op("vector", fn, inc="dve")
                    else:
                        p3.op("vector", fn)
                p3.wait("vector", ev[("sq3", i)])
                ev[("mt3", i)] = p3.op(
                    "vector", lambda e: e.tensor_scalar(
                        stats2[:, 4:8], stats[:, 4:8], 1.0 / D, None, ALU.mult), inc="dve")
                p3.wait("vector", ev[("mt3", i)])
                for j in range(2):
                    fn = lambda e, j=j: e.scalar_tensor_tensor(
                        nv_sb[:, 2 + j:3 + j], stats2[:, 4 + j:5 + j], stats2[:, 4 + j:5 + j],
                        stats2[:, 6 + j:7 + j], ALU.mult, ALU.subtract)
                    if j == 1:
                        ev[("nv3", i)] = p3.op("vector", fn, inc="dve")
                    else:
                        p3.op("vector", fn)
                pending_tail.append(i)
            while pending_tail:
                p3_tail(pending_tail.pop(0))
            p3.wait("scalar", ("dod", cnt["dod"]))
            p3.emit(nc)

    return nc


def host_consts():
    import ml_dtypes
    ident = np.eye(128, dtype=ml_dtypes.bfloat16)
    tri = np.triu(np.ones((128, 128))).astype(ml_dtypes.bfloat16)
    ones_c = np.ones((128, 1), dtype=ml_dtypes.bfloat16)
    ones_r = np.ones((1, 128), dtype=ml_dtypes.bfloat16)
    return ident, tri, ones_c, ones_r


def make_in_map(x_core, Wq, bq, Wk, bk, Wv, bv, w1, b1, w2, b2):
    ident, tri, ones_c, ones_r = host_consts()
    return {
        "x": np.ascontiguousarray(x_core),
        "Wq": np.ascontiguousarray(Wq), "Wk": np.ascontiguousarray(Wk),
        "Wv": np.ascontiguousarray(Wv),
        "bq_l": np.ascontiguousarray(np.asarray(bq).reshape(KC, 128).T),
        "bk_l": np.ascontiguousarray(np.asarray(bk).reshape(KC, 128).T),
        "bv_r": np.ascontiguousarray(np.asarray(bv).reshape(1, D)),
        "w1": np.ascontiguousarray(w1),
        "b1_l": np.ascontiguousarray(np.asarray(b1).reshape(HC, 128).T),
        "w2": np.ascontiguousarray(w2),
        "b2_r": np.ascontiguousarray(np.asarray(b2).reshape(1, D)),
        "ident": ident, "tri": tri, "ones_c": ones_c, "ones_r": ones_r,
    }




from concourse.bass_utils import run_bass_kernel_spmd

_NC_CACHE = {}


def _get_nc():
    if "nc" not in _NC_CACHE:
        _NC_CACHE["nc"] = build_nc(S=8192, dbg=False)
    return _NC_CACHE["nc"]


def kernel(L0_attn_x, L0_attn_Wq, L0_attn_bq, L0_attn_Wk, L0_attn_bk,
           L0_attn_Wv, L0_attn_bv, L0_ffn_w1, L0_ffn_b1, L0_ffn_w2, L0_ffn_b2):
    import ml_dtypes
    x = np.asarray(L0_attn_x, dtype=np.float32)
    B = x.shape[0]
    ident = np.eye(128, dtype=ml_dtypes.bfloat16)
    tri = np.triu(np.ones((128, 128))).astype(ml_dtypes.bfloat16)
    ones_c = np.ones((128, 1), dtype=ml_dtypes.bfloat16)
    ones_r = np.ones((1, 128), dtype=ml_dtypes.bfloat16)
    shared = {
        "Wq": np.ascontiguousarray(np.asarray(L0_attn_Wq, np.float32)),
        "Wk": np.ascontiguousarray(np.asarray(L0_attn_Wk, np.float32)),
        "Wv": np.ascontiguousarray(np.asarray(L0_attn_Wv, np.float32)),
        "bq_l": np.ascontiguousarray(np.asarray(L0_attn_bq, np.float32).reshape(KC, 128).T),
        "bk_l": np.ascontiguousarray(np.asarray(L0_attn_bk, np.float32).reshape(KC, 128).T),
        "bv_r": np.ascontiguousarray(np.asarray(L0_attn_bv, np.float32).reshape(1, D)),
        "w1": np.ascontiguousarray(np.asarray(L0_ffn_w1, np.float32)),
        "b1_l": np.ascontiguousarray(np.asarray(L0_ffn_b1, np.float32).reshape(HC, 128).T),
        "w2": np.ascontiguousarray(np.asarray(L0_ffn_w2, np.float32)),
        "b2_r": np.ascontiguousarray(np.asarray(L0_ffn_b2, np.float32).reshape(1, D)),
        "ident": ident, "tri": tri, "ones_c": ones_c, "ones_r": ones_r,
    }
    in_maps = [dict(shared, x=np.ascontiguousarray(x[b])) for b in range(B)]
    nc = _get_nc()
    res = run_bass_kernel_spmd(nc, in_maps, core_ids=list(range(B)))
    return np.stack([np.asarray(r["out"], dtype=np.float32) for r in res.results], axis=0)


# version 46
# speedup vs baseline: 1.2753x; 1.2753x over previous
import numpy as np
import concourse.bass as bass
import concourse.mybir as mybir

F32 = mybir.dt.float32
BF16 = mybir.dt.bfloat16
AF = mybir.ActivationFunctionType
ALU = mybir.AluOpType

D = 1024
KC = 8
H = 4096
HC = 32
EPS = 1e-5

SEM_NAMES = ("pe", "act", "dve", "pl",
             "dcst", "dsa", "dsb", "dsc", "dsd", "dx", "dxb", "dfa", "dfb", "dlna", "dlnb", "dx2a", "dx2b",
             "dv0", "dv1", "dv2", "dv3", "dv4", "dv5", "dv6", "dv7",
             "dvs0", "dvs1", "dxf0", "dxf1", "dln", "dod", "dwl", "dwb")


class Prog:
    ENGS = ("sync", "tensor", "vector", "scalar", "gpsimd")

    def __init__(self, sems, cnt, marks=None):
        self.sems = sems
        self.cnt = cnt
        self.marks = marks if marks is not None else {e: {} for e in self.ENGS}
        self.ops = {e: [] for e in self.ENGS}

    def wait(self, eng, evt):
        if evt is None:
            return
        sem_name, val = evt
        if val <= 0:
            return
        if self.marks[eng].get(sem_name, 0) >= val:
            return
        self.marks[eng][sem_name] = val
        s = self.sems[sem_name]
        self.ops[eng].append(lambda e, s=s, v=val: e.wait_ge(s, v))

    def op(self, eng, fn, inc=None, amt=1):
        if inc is None:
            self.ops[eng].append(fn)
            return None
        self.cnt[inc] += amt
        s = self.sems[inc]
        self.ops[eng].append(lambda e, fn=fn, s=s, a=amt: fn(e).then_inc(s, a))
        return (inc, self.cnt[inc])

    def dma(self, eng, fn, stream):
        return self.op(eng, fn, inc=stream, amt=16)

    def emit(self, nc):
        with nc.Block() as block:
            for eng in self.ENGS:
                lst = self.ops[eng]
                if not lst:
                    continue
                getattr(block, eng)(lambda e, lst=lst: [f(e) for f in lst])
        merged = {}
        for e in self.ENGS:
            for s, v in self.marks[e].items():
                if merged.get(s, 0) < v:
                    merged[s] = v
        for e in self.ENGS:
            self.marks[e] = dict(merged)


def build_nc(S=8192, dbg=False):
    NP1 = S // 256
    NQ = S // 256
    NP3 = S // 256

    nc = bass.Bass("TRN2", target_bir_lowering=False)

    x_in = nc.dram_tensor("x", [S, D], F32, kind="ExternalInput")
    Wq = nc.dram_tensor("Wq", [D, D], F32, kind="ExternalInput")
    Wk = nc.dram_tensor("Wk", [D, D], F32, kind="ExternalInput")
    Wv = nc.dram_tensor("Wv", [D, D], F32, kind="ExternalInput")
    bq_l = nc.dram_tensor("bq_l", [128, KC], F32, kind="ExternalInput")
    bk_l = nc.dram_tensor("bk_l", [128, KC], F32, kind="ExternalInput")
    bv_r = nc.dram_tensor("bv_r", [1, D], F32, kind="ExternalInput")
    w1 = nc.dram_tensor("w1", [D, H], F32, kind="ExternalInput")
    b1_l = nc.dram_tensor("b1_l", [128, HC], F32, kind="ExternalInput")
    w2 = nc.dram_tensor("w2", [H, D], F32, kind="ExternalInput")
    b2_r = nc.dram_tensor("b2_r", [1, D], F32, kind="ExternalInput")
    ident_in = nc.dram_tensor("ident", [128, 128], BF16, kind="ExternalInput")
    tri_in = nc.dram_tensor("tri", [128, 128], BF16, kind="ExternalInput")
    ones_c_in = nc.dram_tensor("ones_c", [128, 1], BF16, kind="ExternalInput")
    ones_r_in = nc.dram_tensor("ones_r", [1, 128], BF16, kind="ExternalInput")

    out = nc.dram_tensor("out", [S, D], F32, kind="ExternalOutput")
    sc_kind = "ExternalOutput" if dbg else "Internal"
    V_sc = nc.dram_tensor("V_sc", [S, D], BF16, kind=sc_kind)
    qfm_sc = nc.dram_tensor("qfm_sc", [128, NP1, KC, 256], BF16, kind=sc_kind)
    ln1_sc = nc.dram_tensor("ln1_sc", [S, D], F32, kind=sc_kind)
    w1bf_sc = nc.dram_tensor("w1bf_sc", [128, KC, H], BF16, kind="Internal")
    w2bf_sc = nc.dram_tensor("w2bf_sc", [128, HC, D], BF16, kind="Internal")
    z_dbg = nc.dram_tensor("z_dbg", [S, D], F32, kind=sc_kind) if dbg else None
    rl_dbg = nc.dram_tensor("rl_dbg", [S // 256, 128, 4], F32, kind=sc_kind) if dbg else None

    x_t = x_in.rearrange("(n s p) d -> n p s d", p=128, s=2)
    v_t = V_sc.rearrange("(n p) d -> n p d", p=128)
    ln_t = ln1_sc.rearrange("(n s p) d -> n p s d", p=128, s=2)
    z_t = z_dbg.rearrange("(n s p) d -> n p s d", p=128, s=2) if dbg else None
    out_t = out.rearrange("(n s p) d -> n p s d", p=128, s=2)

    cnt = {s: 0 for s in SEM_NAMES}
    marks = {e: {} for e in Prog.ENGS}

    from contextlib import ExitStack
    es = ExitStack()
    with es:
        sems = {s: es.enter_context(nc.semaphore(f"s_{s}")) for s in SEM_NAMES}
        ident = es.enter_context(nc.sbuf_tensor("ident_sb", [128, 128], BF16))
        tri = es.enter_context(nc.sbuf_tensor("tri_sb", [128, 128], BF16))
        ones_c = es.enter_context(nc.sbuf_tensor("ones_c_sb", [128, 1], BF16))
        ones_r33 = es.enter_context(nc.sbuf_tensor("ones_r_sb", [33, 128], BF16))
        ones_r = ones_r33[0:1, :]
        ones_r32 = ones_r33[32:33, :]
        bq_sb = es.enter_context(nc.sbuf_tensor("bq_sb", [128, KC], F32))
        bk_sb = es.enter_context(nc.sbuf_tensor("bk_sb", [128, KC], F32))
        b1_sb = es.enter_context(nc.sbuf_tensor("b1_sb", [128, HC], F32))
        bvb2_bf = es.enter_context(nc.sbuf_tensor("bvb2_bf", [33, D], BF16))
        bv_bf = bvb2_bf[0:1, :]
        b2_bf = bvb2_bf[32:33, :]
        stats = es.enter_context(nc.sbuf_tensor("stats", [128, 8], F32))
        stats2 = es.enter_context(nc.sbuf_tensor("stats2", [128, 8], F32))
        nv_sb = es.enter_context(nc.sbuf_tensor("nv_sb", [128, 4], F32))
        sd_sb = es.enter_context(nc.sbuf_tensor("sd_sb", [128, 4], F32))
        rs_sb = es.enter_context(nc.sbuf_tensor("rs_sb", [128, 4], F32))
        rl_sb = es.enter_context(nc.sbuf_tensor("rl_sb", [128, 4], F32))
        eps_sb = es.enter_context(nc.sbuf_tensor("eps_sb", [128, 1], F32))

        ev = {}

        def prev2(name, i, c, per):
            if c - 2 >= 0:
                return ev.get((name, i, c - 2))
            if i < 0 or i == 0:
                return None
            return ev.get((name, i - 1, per + (c - 2)))

        with ExitStack() as ab:
            K_fm = ab.enter_context(nc.sbuf_tensor("K_fm", [128, KC, S], BF16))
            WqT = ab.enter_context(nc.sbuf_tensor("WqT", [128, KC, D], BF16))

            with ExitStack() as a:
                WkT = a.enter_context(nc.sbuf_tensor("WkT", [128, KC, D], BF16))
                WvT = a.enter_context(nc.sbuf_tensor("WvT", [128, KC, D], BF16))
                bv_bc = a.enter_context(nc.sbuf_tensor("bv_bc", [128, D], BF16))
                slab_scope = ExitStack()
                slab = [slab_scope.enter_context(nc.sbuf_tensor(f"slab{j}", [128, KC, 128], F32)) for j in range(3)]
                slab_bf = [slab_scope.enter_context(nc.sbuf_tensor(f"slabbf{j}", [128, KC, 128], BF16)) for j in range(3)]
                bvb2_f32 = slab_scope.enter_context(nc.sbuf_tensor("bvb2_f32", [33, D], F32))
                bv_f32 = bvb2_f32[0:1, :]
                b2_f32 = bvb2_f32[32:33, :]
                ps_t = [a.enter_context(nc.psum_tensor(f"ps_t{j}", [128, 1024], BF16)) for j in range(2)]
                ps_k = [a.enter_context(nc.psum_tensor(f"ps_k{j}", [128, 512], F32)) for j in range(2)]
                ps_v = [a.enter_context(nc.psum_tensor(f"ps_v{j}", [128, 1024], F32)) for j in range(2)]

                p0 = Prog(sems, cnt, marks)
                for dst, src in ((ident, ident_in), (tri, tri_in), (ones_c, ones_c_in),
                                 (ones_r, ones_r_in), (ones_r32, ones_r_in),
                                 (bq_sb, bq_l), (bk_sb, bk_l),
                                 (b1_sb, b1_l), (bv_f32, bv_r), (b2_f32, b2_r)):
                    ev["consts"] = p0.dma("sync", lambda e, d=dst, s=src: e.dma_start(d[:, :], s[:, :]), "dcst")
                p0.wait("vector", ev["consts"])
                p0.op("vector", lambda e: e.memset(eps_sb[:, :], EPS))
                p0.op("vector", lambda e: e.tensor_copy(bv_bf[:, :], bv_f32[:, :]))
                ev["bcast"] = p0.op("vector", lambda e: e.tensor_copy(b2_bf[:, :], b2_f32[:, :]), inc="dve")
                p0.wait("tensor", ev["bcast"])
                for oh in range(2):
                    fn = lambda e, oh=oh: e.matmul(
                        ps_v[0][:, oh * 512:(oh + 1) * 512], ones_r[:, :],
                        bv_bf[:, oh * 512:(oh + 1) * 512], start=True, stop=True)
                    ev["bvmm"] = p0.op("tensor", fn, inc="pe" if oh == 1 else None)
                p0.wait("scalar", ev["bvmm"])
                ev["bvbc"] = p0.op(
                    "scalar", lambda e: e.activation(bv_bc[:, :], ps_v[0][:, :], AF.Identity),
                    inc="act")

                def w_prep(p, Wsrc, Wdst, rounds):
                    for r in range(KC):
                        rg = len(rounds)
                        par = rg % 3
                        prev = rounds[rg - 3] if rg >= 3 else {}
                        ldeng = "sync"
                        p.wait(ldeng, prev.get("cast"))
                        e_ld = p.dma(
                            ldeng,
                            lambda e, r=r, par=par, W=Wsrc: e.dma_start(
                                slab[par][:, :, :].rearrange("p a b -> p (a b)"),
                                W[r * 128:(r + 1) * 128, :]),
                            ("dsa", "dsb", "dsc")[par])
                        p.wait("vector", e_ld)
                        p.wait("vector", prev.get("tr"))
                        e_cast = p.op(
                            "vector", lambda e, par=par: e.tensor_copy(slab_bf[par][:, :, :], slab[par][:, :, :]),
                            inc="dve")
                        p.wait("tensor", e_cast)
                        p.wait("tensor", (rounds[rg - 2] if rg >= 2 else {}).get("evac"))
                        e_tr = None
                        for k in range(KC):
                            fn = lambda e, k=k, par=par: e.transpose(
                                ps_t[par % 2][:, k * 128:(k + 1) * 128],
                                slab_bf[par][:, :, :].rearrange("p a b -> p (a b)")[:, k * 128:(k + 1) * 128],
                                ident[:, :])
                            if k == KC - 1:
                                e_tr = p.op("tensor", fn, inc="pe")
                            else:
                                p.op("tensor", fn)
                        p.wait("vector", e_tr)
                        e_evac = p.op(
                            "vector", lambda e, r=r, par=par, Wd=Wdst: e.tensor_copy(
                                Wd[:, :, r * 128:(r + 1) * 128],
                                ps_t[par % 2][:, :].rearrange("p (k t) -> p k t", k=KC)),
                            inc="dve")
                        rounds.append({"cast": e_cast, "tr": e_tr, "evac": e_evac})
                w_rounds = []
                w_prep(p0, Wk, WkT, w_rounds)
                w_prep(p0, Wv, WvT, w_rounds)
                w_prep(p0, Wq, WqT, w_rounds)
                p0.emit(nc)
                slab_scope.close()
                x_sb = a.enter_context(nc.sbuf_tensor("x_sb", [128, 2, D], F32))
                xbf = a.enter_context(nc.sbuf_tensor("xbf", [128, 2, D], BF16))
                x_fm = [a.enter_context(nc.sbuf_tensor(f"x_fm{j}", [128, KC, 256], BF16)) for j in range(2)]
                v_stage = [a.enter_context(nc.sbuf_tensor(f"v_st{j}", [128, D], BF16)) for j in range(2)]

                p1 = Prog(sems, cnt, marks)

                def p1_xld_cast(i):
                    if i >= NP1:
                        return
                    p1.wait("sync", ev.get(("cast1", i - 1)))
                    ev[("xld", i)] = p1.dma(
                        "sync", lambda e, i=i: e.dma_start(x_sb[:, :, :], x_t[i]), "dx")
                    p1.wait("vector", ev[("xld", i)])
                    p1.wait("vector", ev.get(("transp", i - 1, 1)))
                    ev[("cast1", i)] = p1.op(
                        "vector", lambda e: e.tensor_copy(
                            xbf[:, :, :].rearrange("p s d -> p (s d)"),
                            x_sb[:, :, :].rearrange("p s d -> p (s d)")), inc="dve")

                def p1_prep(j):
                    if j >= NP1:
                        return
                    parj = j % 2
                    for s2 in range(2):
                        p1.wait("tensor", ev[("cast1", j)])
                        p1.wait("tensor", ev.get(("tevac", j - 1, s2)))
                        for k in range(KC):
                            fn = lambda e, s2=s2, k=k: e.transpose(
                                ps_t[s2][:, k * 128:(k + 1) * 128], xbf[:, s2, k * 128:(k + 1) * 128], ident[:, :])
                            if k == KC - 1:
                                ev[("transp", j, s2)] = p1.op("tensor", fn, inc="pe")
                            else:
                                p1.op("tensor", fn)
                        p1.wait("vector", ev[("transp", j, s2)])
                        p1.wait("vector", ev.get(("vproj", j - 2, 1)))
                        ev[("tevac", j, s2)] = p1.op(
                            "vector", lambda e, s2=s2, parj=parj: e.tensor_copy(
                                x_fm[parj][:, :, s2 * 128:(s2 + 1) * 128],
                                ps_t[s2][:, :].rearrange("p (k t) -> p k t", k=KC)), inc="dve")

                p1_xld_cast(0)
                p1_prep(0)
                for i in range(NP1):
                    par = i % 2
                    p1_xld_cast(i + 1)
                    for c in range(2 * KC):
                        if c == 8:
                            p1_prep(i + 1)
                        if c == 0:
                            p1.wait("tensor", ev[("tevac", i, 1)])
                        p1.wait("tensor", prev2("kevac", i, c, 2 * KC))
                        WT = WkT if c < KC else WqT
                        cc = c % KC
                        for k in range(KC):
                            fn = lambda e, c=c, cc=cc, k=k, par=par, WT=WT: e.matmul(
                                ps_k[c % 2][:, 0:256], WT[:, k, cc * 128:(cc + 1) * 128],
                                x_fm[par][:, k, :], start=(k == 0), stop=(k == KC - 1))
                            if k == KC - 1:
                                ev[("kproj", i, c)] = p1.op("tensor", fn, inc="pe")
                            else:
                                p1.op("tensor", fn)
                        p1.wait("scalar", ev[("kproj", i, c)])
                        if c < KC:
                            ev[("kevac", i, c)] = p1.op(
                                "scalar", lambda e, i=i, c=c: e.activation(
                                    K_fm[:, c, i * 256:(i + 1) * 256], ps_k[c % 2][:, 0:256],
                                    AF.Identity, bias=bk_sb[:, c:c + 1], scale=1.0), inc="act")
                        else:
                            hh = cc // 4
                            if cc % 4 == 0:
                                p1.wait("scalar", ev.get(("vdma", i - 1, hh)))
                            ev[("kevac", i, c)] = p1.op(
                                "scalar", lambda e, c=c, cc=cc, hh=hh: e.activation(
                                    v_stage[hh][:, (cc % 4) * 256:(cc % 4 + 1) * 256],
                                    ps_k[c % 2][:, 0:256],
                                    AF.Identity, bias=bq_sb[:, cc:cc + 1], scale=1.0), inc="act")
                            if cc % 4 == 3:
                                p1.wait("scalar", ev[("kevac", i, c)])
                                ev[("qdma", i, hh)] = p1.dma(
                                    "scalar", lambda e, i=i, hh=hh: e.dma_start(
                                        qfm_sc[:, i, 4 * hh:4 * hh + 4, :],
                                        v_stage[hh][:, :].rearrange("p (a b) -> p a b", a=4)),
                                    "dxf0" if hh == 0 else "dxf1")
                    for s2 in range(2):
                        p1.wait("tensor", ev.get(("vevac", i - 1, s2)))
                        for oh in range(2):
                            for k in range(KC):
                                fn = lambda e, s2=s2, oh=oh, k=k, par=par: e.matmul(
                                    ps_v[s2][:, oh * 512:(oh + 1) * 512],
                                    x_fm[par][:, k, s2 * 128:(s2 + 1) * 128],
                                    WvT[:, k, oh * 512:(oh + 1) * 512],
                                    start=(k == 0), stop=(k == KC - 1))
                                if oh == 1 and k == KC - 1:
                                    ev[("vproj", i, s2)] = p1.op("tensor", fn, inc="pe")
                                else:
                                    p1.op("tensor", fn)
                        p1.wait("scalar", ev[("vproj", i, s2)])
                        p1.wait("scalar", ev.get(("vdma", i - 1, s2)))
                        p1.wait("scalar", ev[("qdma", i, s2)])
                        ev[("vevac", i, s2)] = p1.op(
                            "scalar", lambda e, s2=s2: e.activation(
                                v_stage[s2][:, :], ps_v[s2][:, :], AF.Identity), inc="act")
                        p1.wait("gpsimd", ev[("vevac", i, s2)])
                        ev[("bvadd", i, s2)] = p1.op(
                            "gpsimd", lambda e, s2=s2: e.tensor_tensor(
                                v_stage[s2][:, :], v_stage[s2][:, :], bv_bc[:, :], ALU.add),
                            inc="pl")
                        p1.wait("gpsimd", ev[("bvadd", i, s2)])
                        ev[("vdma", i, s2)] = p1.dma(
                            "gpsimd", lambda e, i=i, s2=s2: e.dma_start(
                                v_t[2 * i + s2], v_stage[s2][:, :]), "dvs0" if s2 == 0 else "dvs1")
                for s in ("dvs0", "dvs1", "dxf0", "dxf1"):
                    p1.wait("scalar", (s, cnt[s]))
                p1.emit(nc)

            with ExitStack() as b:
                x2_sb = [b.enter_context(nc.sbuf_tensor(f"x2_sb{j}", [128, 2, D], F32)) for j in range(2)]
                Q_sb = [b.enter_context(nc.sbuf_tensor(f"Q_sb{j}", [128, KC, 256], BF16)) for j in range(2)]
                P_sb = [b.enter_context(nc.sbuf_tensor(f"P_sb{j}", [128, 256], BF16)) for j in range(4)]
                V_ring = [b.enter_context(nc.sbuf_tensor(f"V_ring{j}", [128, D], BF16)) for j in range(8)]
                ln1_sb = b.enter_context(nc.sbuf_tensor("ln1_sb", [128, 2, D], F32))
                wsl = b.enter_context(nc.sbuf_tensor("wsl", [128, 1024], F32))
                wslb = b.enter_context(nc.sbuf_tensor("wslb", [128, 1024], BF16))
                ps_s = [b.enter_context(nc.psum_tensor(f"ps_s{j}", [128, 512], F32)) for j in range(2)]
                ps_z = [b.enter_context(nc.psum_tensor(f"ps_z{j}", [128, 1024], F32)) for j in range(2)]
                ps_l = b.enter_context(nc.psum_tensor("ps_l", [128, 512], F32))

                p2 = Prog(sems, cnt, marks)
                g = 0
                gev = {}

                def p2_loads(i):
                    if i >= NQ:
                        return
                    par = i % 2
                    p2.wait("gpsimd", ev.get(("tdone", i - 2)))
                    ev[("qfm", i)] = p2.dma(
                        "gpsimd", lambda e, i=i, par=par: e.dma_start(
                            Q_sb[par][:, :, :], qfm_sc[:, i, :, :]), "dfa" if par == 0 else "dfb")
                    p2.wait("gpsimd", ev.get(("apply", i - 2)))
                    ev[("x2", i)] = p2.dma(
                        "gpsimd", lambda e, i=i, par=par: e.dma_start(x2_sb[par][:, :, :], x_t[i]),
                        "dx2a" if par == 0 else "dx2b")

                pending2 = []

                def p2_tail(i):
                    par_t = i % 2
                    p2.wait("scalar", ev[("nv", i)])
                    ev[("sd", i)] = p2.op(
                        "scalar", lambda e: e.activation(
                            sd_sb[:, 0:2], nv_sb[:, 0:2], AF.Ln, bias=eps_sb[:, 0:1], scale=-1.0), inc="act")
                    p2.wait("scalar", ev[("sd", i)])
                    ev[("rs", i)] = p2.op(
                        "scalar", lambda e: e.activation(
                            rs_sb[:, 0:2], sd_sb[:, 0:2], AF.Exp, scale=-0.5), inc="act")
                    p2.wait("vector", ev[("rs", i)])
                    for j in range(2):
                        fn = lambda e, j=j, par_t=par_t: e.tensor_scalar(
                            ln1_sb[:, j, :], x2_sb[par_t][:, j, :], stats2[:, j:j + 1], rs_sb[:, j:j + 1],
                            ALU.subtract, ALU.mult)
                        if j == 1:
                            ev[("apply", i)] = p2.op("vector", fn, inc="dve")
                        else:
                            p2.op("vector", fn)
                    p2.wait("gpsimd", ev[("apply", i)])
                    ev[("lndma", i)] = p2.dma(
                        "gpsimd", lambda e, i=i: e.dma_start(ln_t[i], ln1_sb[:, :, :]), "dln")

                p2_loads(0)

                for i in range(NQ):
                    par = i % 2
                    d0 = 2 * i
                    nch = 2 * i + 2

                    def v_load(c):
                        gg = g + c
                        p2.wait("sync", gev.get(("pv", gg - 8)))
                        gev[("vld", gg)] = p2.dma(
                            "sync", lambda e, c=c, gg=gg: e.dma_start(
                                V_ring[gg % 8][:, :], v_t[c]), f"dv{gg % 8}")
                    for c in range(min(8, nch)):
                        v_load(c)

                    def scores(c):
                        gg = g + c
                        qlo = 0 if c <= d0 else 128
                        if c == 0:
                            p2.wait("tensor", ev[("qfm", i)])
                        p2.wait("tensor", gev.get(("exp", gg - 2)))
                        for k in range(KC):
                            fn = lambda e, c=c, k=k, qlo=qlo, par=par, gg=gg: e.matmul(
                                ps_s[gg % 2][:, qlo:256], K_fm[:, k, c * 128:(c + 1) * 128],
                                Q_sb[par][:, k, qlo:256], start=(k == 0), stop=(k == KC - 1))
                            if k == KC - 1:
                                gev[("sc", gg)] = p2.op("tensor", fn, inc="pe")
                            else:
                                p2.op("tensor", fn)
                        p2.wait("scalar", gev[("sc", gg)])
                        p2.wait("scalar", gev.get(("pv", gg - 4)))
                        gev[("exp", gg)] = p2.op(
                            "scalar", lambda e, gg=gg, qlo=qlo: e.activation(
                                P_sb[gg % 4][:, qlo:256], ps_s[gg % 2][:, qlo:256],
                                AF.Exp, scale=1.0 / 32.0), inc="act")
                        if c >= d0:
                            j = c - d0
                            p2.wait("vector", gev[("exp", gg)])
                            gev[("mask", gg)] = p2.op(
                                "vector", lambda e, gg=gg, j=j: e.tensor_mul(
                                    P_sb[gg % 4][:, j * 128:(j + 1) * 128],
                                    P_sb[gg % 4][:, j * 128:(j + 1) * 128], tri[:, :]), inc="dve")

                    def pv(c):
                        gg = g + c
                        for j in range(2):
                            if c > d0 + j:
                                continue
                            if c - d0 == j:
                                p2.wait("tensor", gev[("mask", gg)])
                            else:
                                p2.wait("tensor", gev[("exp", gg)])
                            p2.wait("tensor", gev[("vld", gg)])
                            if c == 0:
                                p2.wait("tensor", ev.get((f"z{j}", i - 1)))
                            lhs = lambda gg=gg, j=j: P_sb[gg % 4][:, j * 128:(j + 1) * 128]
                            stz = (c == d0 + j)
                            stl = (c == d0 + 1 and j == 1)
                            for oh in range(2):
                                p2.op("tensor", lambda e, gg=gg, j=j, oh=oh, c=c, lhs=lhs, stz=stz: e.matmul(
                                    ps_z[j][:, oh * 512:(oh + 1) * 512], lhs(),
                                    V_ring[gg % 8][:, oh * 512:(oh + 1) * 512],
                                    start=(c == 0), stop=stz))
                            fn = lambda e, gg=gg, j=j, c=c, lhs=lhs, stl=stl: e.matmul(
                                ps_l[:, j:j + 1], lhs(), ones_c[:, :],
                                start=(c == 0 and j == 0), stop=stl)
                            if j == 1:
                                gev[("pv", gg)] = p2.op("tensor", fn, inc="pe")
                            else:
                                p2.op("tensor", fn)

                    for c in range(nch):
                        if c == 3 and pending2:
                            p2_tail(pending2.pop(0))
                        if c == 4:
                            p2_loads(i + 1)
                        scores(c)
                        if c > 0:
                            pv(c - 1)
                            if c + 7 < nch:
                                v_load(c + 7)
                    if pending2:
                        p2_tail(pending2.pop(0))
                    if nch <= 4:
                        p2_loads(i + 1)
                    pv(nch - 1)
                    g += nch
                    ev[("tdone", i)] = gev[("pv", g - 1)]

                    p2.wait("vector", gev[("pv", g - 1)])
                    ev[("rl", i)] = p2.op(
                        "vector", lambda e: e.reciprocal(rl_sb[:, 0:2], ps_l[:, 0:2]), inc="dve")
                    p2.wait("vector", ev[("rl", i)])
                    p2.wait("vector", ev[("x2", i)])
                    for j in range(2):
                        fn = lambda e, j=j, par=par: e.scalar_tensor_tensor(
                            x2_sb[par][:, j, :], ps_z[j][:, :], rl_sb[:, j:j + 1], x2_sb[par][:, j, :],
                            ALU.mult, ALU.add, accum_out=stats[:, j:j + 1])
                        ev[(f"z{j}", i)] = p2.op("vector", fn, inc="dve")
                    if dbg:
                        p2.wait("scalar", ev[("z1", i)])
                        p2.dma("scalar", lambda e, i=i, par=par: e.dma_start(z_t[i], x2_sb[par][:, :, :]), "dod")
                        p2.dma("scalar", lambda e, i=i: e.dma_start(rl_dbg[i], rl_sb[:, :]), "dod")
                    p2.wait("vector", ev[("z1", i)])
                    p2.wait("vector", ev.get(("lndma", i - 1)))
                    for j in range(2):
                        fn = lambda e, j=j, par=par: e.scalar_tensor_tensor(
                            ln1_sb[:, j, :], x2_sb[par][:, j, :], 1.0, x2_sb[par][:, j, :],
                            ALU.mult, ALU.mult, accum_out=stats[:, 2 + j:3 + j])
                        if j == 1:
                            ev[("sq", i)] = p2.op("vector", fn, inc="dve")
                        else:
                            p2.op("vector", fn)
                    p2.wait("vector", ev[("sq", i)])
                    ev[("mt", i)] = p2.op(
                        "vector", lambda e: e.tensor_scalar(
                            stats2[:, 0:4], stats[:, 0:4], 1.0 / D, None, ALU.mult), inc="dve")
                    p2.wait("vector", ev[("mt", i)])
                    for j in range(2):
                        fn = lambda e, j=j: e.scalar_tensor_tensor(
                            nv_sb[:, j:j + 1], stats2[:, j:j + 1], stats2[:, j:j + 1],
                            stats2[:, 2 + j:3 + j], ALU.mult, ALU.subtract)
                        if j == 1:
                            ev[("nv", i)] = p2.op("vector", fn, inc="dve")
                        else:
                            p2.op("vector", fn)
                    pending2.append(i)
                    rpt = -(-64 // NQ)
                    for r in range(rpt * i, rpt * (i + 1)):
                        if r < 32:
                            k_, q_ = r // 4, r % 4
                            wsrc = w1[k_ * 128:(k_ + 1) * 128, q_ * 1024:(q_ + 1) * 1024]
                            wdst = w1bf_sc[:, k_, q_ * 1024:(q_ + 1) * 1024]
                        elif r < 64:
                            hc_ = r - 32
                            wsrc = w2[hc_ * 128:(hc_ + 1) * 128, :]
                            wdst = w2bf_sc[:, hc_, :]
                        else:
                            continue
                        p2.wait("gpsimd", ev.get(("wcst1", r - 1)))
                        ev[("wld1", r)] = p2.dma(
                            "gpsimd", lambda e, wsrc=wsrc: e.dma_start(wsl[:, :], wsrc), "dwl")
                        p2.wait("vector", ev[("wld1", r)])
                        p2.wait("vector", ev.get(("wst", r - 1)))
                        ev[("wcst1", r)] = p2.op(
                            "vector", lambda e: e.tensor_copy(wslb[:, :], wsl[:, :]), inc="dve")
                        p2.wait("gpsimd", ev[("wcst1", r)])
                        ev[("wst", r)] = p2.dma(
                            "gpsimd", lambda e, wdst=wdst: e.dma_start(wdst, wslb[:, :]), "dwb")
                while pending2:
                    p2_tail(pending2.pop(0))
                p2.wait("gpsimd", ("dln", cnt["dln"]))
                p2.wait("gpsimd", ("dwb", cnt["dwb"]))
                p2.emit(nc)

        with ExitStack() as cs:
            w1_sb = cs.enter_context(nc.sbuf_tensor("w1_sb", [128, KC, H], BF16))
            w2_sb = cs.enter_context(nc.sbuf_tensor("w2_sb", [128, HC, D], BF16))
            b2_bc = cs.enter_context(nc.sbuf_tensor("b2_bc", [128, D], F32))
            ln_sb = [cs.enter_context(nc.sbuf_tensor(f"ln_sb{j}", [128, 2, D], F32)) for j in range(2)]
            lnbf = cs.enter_context(nc.sbuf_tensor("lnbf", [128, 2, D], BF16))
            lnfm = [cs.enter_context(nc.sbuf_tensor(f"lnfm{j}", [128, KC, 256], BF16)) for j in range(2)]
            H_sb = cs.enter_context(nc.sbuf_tensor("H_sb", [128, HC, 256], BF16))
            out_sb = cs.enter_context(nc.sbuf_tensor("out_sb", [128, 2, D], F32))
            ps_t3 = [cs.enter_context(nc.psum_tensor(f"ps_t3{j}", [128, 1024], BF16)) for j in range(2)]
            ps_h = [cs.enter_context(nc.psum_tensor(f"ps_h{j}", [128, 512], F32)) for j in range(2)]
            ps_y = [cs.enter_context(nc.psum_tensor(f"ps_y{j}", [128, 1024], F32)) for j in range(2)]

            p3 = Prog(sems, cnt, marks)
            ev[("lnld", 0)] = p3.dma(
                "sync", lambda e: e.dma_start(ln_sb[0][:, :, :], ln_t[0]), "dlna")
            for oh in range(2):
                fn = lambda e, oh=oh: e.matmul(
                    ps_y[0][:, oh * 512:(oh + 1) * 512], ones_r32[:, :],
                    b2_bf[:, oh * 512:(oh + 1) * 512], start=True, stop=True)
                ev["b2mm"] = p3.op("tensor", fn, inc="pe" if oh == 1 else None)
            p3.wait("vector", ev["b2mm"])
            ev["b2bc"] = p3.op(
                "vector", lambda e: e.tensor_copy(b2_bc[:, :], ps_y[0][:, :]), inc="dve")
            p3.wait("sync", ("dwb", cnt["dwb"]))
            for r in range(8):
                dstream = ("dsa", "dsb", "dsc", "dsd", "dwl", "dwb", "dcst", "dx")[r]
                if r >= 2:
                    p3.wait("sync", ev[("wld3", r - 2)])
                if r < 4:
                    fn = lambda e, r=r: e.dma_start(
                        w1_sb[:, :, r * 1024:(r + 1) * 1024],
                        w1bf_sc[:, :, r * 1024:(r + 1) * 1024])
                else:
                    g_ = r - 4
                    fn = lambda e, g_=g_: e.dma_start(
                        w2_sb[:, 8 * g_:8 * g_ + 8, :], w2bf_sc[:, 8 * g_:8 * g_ + 8, :])
                ev[("wld3", r)] = p3.dma("sync", fn, dstream)

            def p3_lnld(i):
                if i >= NP3:
                    return
                par = i % 2
                dstream = "dlna" if par == 0 else "dlnb"
                p3.wait("sync", ev.get(("z3", i - 2)))
                ev[("lnld", i)] = p3.dma(
                    "sync", lambda e, i=i, par=par: e.dma_start(ln_sb[par][:, :, :], ln_t[i]),
                    dstream)

            def p3_prep(i):
                if i >= NP3:
                    return
                par = i % 2
                p3.wait("vector", ev[("lnld", i)])
                p3.wait("vector", ev.get(("t3", i - 1, 1)))
                ev[("cast3", i)] = p3.op(
                    "vector", lambda e, par=par: e.tensor_copy(
                        lnbf[:, :, :].rearrange("p s d -> p (s d)"),
                        ln_sb[par][:, :, :].rearrange("p s d -> p (s d)")), inc="dve")
                p3.wait("vector", ev[("cast3", i)])
                if i == 0:
                    p3.wait("vector", ev["b2bc"])
                for j in range(2):
                    fn = lambda e, j=j, par=par: e.tensor_tensor(
                        ln_sb[par][:, j, :], ln_sb[par][:, j, :], b2_bc[:, :], ALU.add)
                    if j == 1:
                        ev[("b2add", i)] = p3.op("vector", fn, inc="dve")
                    else:
                        p3.op("vector", fn)
                for s2 in range(2):
                    p3.wait("tensor", ev[("cast3", i)])
                    p3.wait("tensor", ev.get(("t3evac", i - 1, s2)))
                    for k in range(KC):
                        fn = lambda e, s2=s2, k=k: e.transpose(
                            ps_t3[s2][:, k * 128:(k + 1) * 128], lnbf[:, s2, k * 128:(k + 1) * 128], ident[:, :])
                        if k == KC - 1:
                            ev[("t3", i, s2)] = p3.op("tensor", fn, inc="pe")
                        else:
                            p3.op("tensor", fn)
                    p3.wait("vector", ev[("t3", i, s2)])
                    ev[("t3evac", i, s2)] = p3.op(
                        "vector", lambda e, s2=s2, par=par: e.tensor_copy(
                            lnfm[par][:, :, s2 * 128:(s2 + 1) * 128],
                            ps_t3[s2][:, :].rearrange("p (k t) -> p k t", k=KC)), inc="dve")

            pending_tail = []

            def p3_tail(i):
                p3.wait("scalar", ev[("nv3", i)])
                ev[("sd3", i)] = p3.op(
                    "scalar", lambda e: e.activation(
                        sd_sb[:, 2:4], nv_sb[:, 2:4], AF.Sqrt, bias=eps_sb[:, 0:1], scale=-1.0), inc="act")
                p3.op("scalar", lambda e: e.activation(
                    sd_sb[:, 0:1], eps_sb[:, 0:1], AF.Gelu))
                p3.wait("vector", ev[("sd3", i)])
                ev[("rs3", i)] = p3.op(
                    "vector", lambda e: e.reciprocal(rs_sb[:, 2:4], sd_sb[:, 2:4]), inc="dve")
                p3.wait("vector", ev[("rs3", i)])
                for j in range(2):
                    fn = lambda e, j=j: e.tensor_scalar(
                        out_sb[:, j, :], out_sb[:, j, :], stats2[:, 4 + j:5 + j], rs_sb[:, 2 + j:3 + j],
                        ALU.subtract, ALU.mult)
                    if j == 1:
                        ev[("out3", i)] = p3.op("vector", fn, inc="dve")
                    else:
                        p3.op("vector", fn)
                p3.wait("scalar", ev[("out3", i)])
                ev[("odma", i)] = p3.dma(
                    "scalar", lambda e, i=i: e.dma_start(out_t[i], out_sb[:, :, :]), "dod")

            p3_prep(0)

            for i in range(NP3):
                par = i % 2
                p3_lnld(i + 1)
                for hc in range(HC):
                    if hc == 0:
                        p3.wait("tensor", ev[("t3evac", i, 1)])
                    p3.wait("tensor", prev2("gelu", i, hc, HC))
                    if i == 0:
                        p3.wait("tensor", ev[("wld3", hc // 8)])
                    for k in range(KC):
                        fn = lambda e, hc=hc, k=k, par=par: e.matmul(
                            ps_h[hc % 2][:, 0:256], w1_sb[:, k, hc * 128:(hc + 1) * 128],
                            lnfm[par][:, k, :], start=(k == 0), stop=(k == KC - 1))
                        if k == KC - 1:
                            ev[("f1", i, hc)] = p3.op("tensor", fn, inc="pe")
                        else:
                            p3.op("tensor", fn)
                    p3.wait("scalar", ev[("f1", i, hc)])
                    if hc == 0:
                        p3.wait("scalar", ev.get(("f2", i - 1, 1)))
                    ev[("gelu", i, hc)] = p3.op(
                        "scalar", lambda e, hc=hc: e.activation(
                            H_sb[:, hc, :], ps_h[hc % 2][:, 0:256], AF.Gelu,
                            bias=b1_sb[:, hc:hc + 1], scale=1.0), inc="act")
                p3_prep(i + 1)
                if pending_tail:
                    p3_tail(pending_tail.pop(0))
                for j in range(2):
                    p3.wait("tensor", ev[("gelu", i, HC - 1)])
                    p3.wait("tensor", ev.get(("z3", i - 1)))
                    if i == 0 and j == 0:
                        p3.wait("tensor", ev["b2bc"])
                    for hc in range(HC):
                        if i == 0 and j == 0:
                            p3.wait("tensor", ev[("wld3", 4 + hc // 8)])
                        for oh in range(2):
                            fn = lambda e, j=j, hc=hc, oh=oh: e.matmul(
                                ps_y[j][:, oh * 512:(oh + 1) * 512], H_sb[:, hc, j * 128:(j + 1) * 128],
                                w2_sb[:, hc, oh * 512:(oh + 1) * 512],
                                start=(hc == 0), stop=(hc == HC - 1))
                            if hc == HC - 1 and oh == 1:
                                ev[("f2", i, j)] = p3.op("tensor", fn, inc="pe")
                            else:
                                p3.op("tensor", fn)
                p3.wait("vector", ev[("f2", i, 1)])
                p3.wait("vector", ev.get(("odma", i - 1)))
                p3.wait("vector", ev[("b2add", i)])
                for j in range(2):
                    fn = lambda e, j=j, par=par: e.scalar_tensor_tensor(
                        out_sb[:, j, :], ps_y[j][:, :], 1.0, ln_sb[par][:, j, :],
                        ALU.mult, ALU.add, accum_out=stats[:, 4 + j:5 + j])
                    if j == 1:
                        ev[("z3", i)] = p3.op("vector", fn, inc="dve")
                    else:
                        p3.op("vector", fn)
                p3.wait("vector", ev[("z3", i)])
                for j in range(2):
                    fn = lambda e, j=j: e.scalar_tensor_tensor(
                        lnbf[:, j, :], out_sb[:, j, :], 1.0, out_sb[:, j, :],
                        ALU.mult, ALU.mult, accum_out=stats[:, 6 + j:7 + j])
                    if j == 1:
                        ev[("sq3", i)] = p3.op("vector", fn, inc="dve")
                    else:
                        p3.op("vector", fn)
                p3.wait("vector", ev[("sq3", i)])
                ev[("mt3", i)] = p3.op(
                    "vector", lambda e: e.tensor_scalar(
                        stats2[:, 4:8], stats[:, 4:8], 1.0 / D, None, ALU.mult), inc="dve")
                p3.wait("vector", ev[("mt3", i)])
                for j in range(2):
                    fn = lambda e, j=j: e.scalar_tensor_tensor(
                        nv_sb[:, 2 + j:3 + j], stats2[:, 4 + j:5 + j], stats2[:, 4 + j:5 + j],
                        stats2[:, 6 + j:7 + j], ALU.mult, ALU.subtract)
                    if j == 1:
                        ev[("nv3", i)] = p3.op("vector", fn, inc="dve")
                    else:
                        p3.op("vector", fn)
                pending_tail.append(i)
            while pending_tail:
                p3_tail(pending_tail.pop(0))
            p3.wait("scalar", ("dod", cnt["dod"]))
            p3.emit(nc)

    return nc


def host_consts():
    import ml_dtypes
    ident = np.eye(128, dtype=ml_dtypes.bfloat16)
    tri = np.triu(np.ones((128, 128))).astype(ml_dtypes.bfloat16)
    ones_c = np.ones((128, 1), dtype=ml_dtypes.bfloat16)
    ones_r = np.ones((1, 128), dtype=ml_dtypes.bfloat16)
    return ident, tri, ones_c, ones_r


def make_in_map(x_core, Wq, bq, Wk, bk, Wv, bv, w1, b1, w2, b2):
    ident, tri, ones_c, ones_r = host_consts()
    return {
        "x": np.ascontiguousarray(x_core),
        "Wq": np.ascontiguousarray(Wq), "Wk": np.ascontiguousarray(Wk),
        "Wv": np.ascontiguousarray(Wv),
        "bq_l": np.ascontiguousarray(np.asarray(bq).reshape(KC, 128).T),
        "bk_l": np.ascontiguousarray(np.asarray(bk).reshape(KC, 128).T),
        "bv_r": np.ascontiguousarray(np.asarray(bv).reshape(1, D)),
        "w1": np.ascontiguousarray(w1),
        "b1_l": np.ascontiguousarray(np.asarray(b1).reshape(HC, 128).T),
        "w2": np.ascontiguousarray(w2),
        "b2_r": np.ascontiguousarray(np.asarray(b2).reshape(1, D)),
        "ident": ident, "tri": tri, "ones_c": ones_c, "ones_r": ones_r,
    }




from concourse.bass_utils import run_bass_kernel_spmd

_NC_CACHE = {}


def _get_nc():
    if "nc" not in _NC_CACHE:
        _NC_CACHE["nc"] = build_nc(S=8192, dbg=False)
    return _NC_CACHE["nc"]


def kernel(L0_attn_x, L0_attn_Wq, L0_attn_bq, L0_attn_Wk, L0_attn_bk,
           L0_attn_Wv, L0_attn_bv, L0_ffn_w1, L0_ffn_b1, L0_ffn_w2, L0_ffn_b2):
    import ml_dtypes
    x = np.asarray(L0_attn_x, dtype=np.float32)
    B = x.shape[0]
    ident = np.eye(128, dtype=ml_dtypes.bfloat16)
    tri = np.triu(np.ones((128, 128))).astype(ml_dtypes.bfloat16)
    ones_c = np.ones((128, 1), dtype=ml_dtypes.bfloat16)
    ones_r = np.ones((1, 128), dtype=ml_dtypes.bfloat16)
    shared = {
        "Wq": np.ascontiguousarray(np.asarray(L0_attn_Wq, np.float32)),
        "Wk": np.ascontiguousarray(np.asarray(L0_attn_Wk, np.float32)),
        "Wv": np.ascontiguousarray(np.asarray(L0_attn_Wv, np.float32)),
        "bq_l": np.ascontiguousarray(np.asarray(L0_attn_bq, np.float32).reshape(KC, 128).T),
        "bk_l": np.ascontiguousarray(np.asarray(L0_attn_bk, np.float32).reshape(KC, 128).T),
        "bv_r": np.ascontiguousarray(np.asarray(L0_attn_bv, np.float32).reshape(1, D)),
        "w1": np.ascontiguousarray(np.asarray(L0_ffn_w1, np.float32)),
        "b1_l": np.ascontiguousarray(np.asarray(L0_ffn_b1, np.float32).reshape(HC, 128).T),
        "w2": np.ascontiguousarray(np.asarray(L0_ffn_w2, np.float32)),
        "b2_r": np.ascontiguousarray(np.asarray(L0_ffn_b2, np.float32).reshape(1, D)),
        "ident": ident, "tri": tri, "ones_c": ones_c, "ones_r": ones_r,
    }
    in_maps = [dict(shared, x=np.ascontiguousarray(x[b])) for b in range(B)]
    nc = _get_nc()
    res = run_bass_kernel_spmd(nc, in_maps, core_ids=list(range(B)))
    return np.stack([np.asarray(r["out"], dtype=np.float32) for r in res.results], axis=0)
